# version 26 (final)
# baseline (speedup 1.0000x reference)
import numpy as np
import ml_dtypes
from contextlib import ExitStack

import concourse.bass as bass
import concourse.mybir as mybir
from concourse.bass_utils import run_bass_kernel_spmd

F32 = mybir.dt.float32
BF16 = mybir.dt.bfloat16
I32 = mybir.dt.int32
ALU = mybir.AluOpType
AF = mybir.ActivationFunctionType
AX = mybir.AxisListType

B, TFULL, D, H = 4, 8192, 2048, 1792
NCORE = 4096
TT = 1024
NTILES = NCORE // TT
HALO = 256
DCn = D // 128
HCn = H // 128
EPS = 1e-5
MAGIC = 1597463007.0
RMAGIC = 2130520042.0


def build_program(debug=False):
    nc = bass.Bass("TRN2", target_bir_lowering=False)

    def din(name, shape, dt=F32):
        return nc.dram_tensor(name, list(shape), dt, kind="ExternalInput").ap()

    xin = din("xin", [D, HALO + NCORE])
    wkst = din("wkst", [HCn, 128, D])
    wrst = din("wrst", [HCn, 128, D])
    wgst = din("wgst", [HCn, 128, D])
    wvst = din("wvst", [DCn, 128, H])
    acat_d = din("acat", [128, DCn * 512])
    bcat_d = din("bcat", [128, 4 * 2048])
    wrt_d = din("wrt", [128, 128])
    brf_d = din("brf", [1, 8])
    tmk_d = din("tmk", [128, DCn]); tmr_d = din("tmr", [128, DCn]); tmw_d = din("tmw", [128, DCn])
    lnw_d = din("lnw", [128, DCn]); lnb_d = din("lnb", [128, DCn])
    td_d = din("td", [128, HCn])
    mask_d = din("mask", [128, 1])
    idb_d = din("idb", [128, 128], BF16)
    idf_d = din("idf", [128, 128])
    onesb_d = din("onesb", [128, 128], BF16)
    out_d = nc.dram_tensor("out", [D, NCORE], F32, kind="ExternalOutput").ap()

    with ExitStack() as ctx:
        sb = lambda name, shape, dt: ctx.enter_context(nc.sbuf_tensor(name, shape, dt))
        psb = lambda name, shape: ctx.enter_context(nc.psum_tensor(name, shape, F32))

        KREG = sb("kreg", [128, 15360], F32)
        ARENA = sb("arena", [128, 24576], F32)
        WS = [sb("ws0", [128, 1024], F32), sb("ws1", [128, 1024], F32)]
        ACAT = sb("acatb", [128, DCn * 512], BF16)
        BCAT = sb("bcatb", [128, 4 * 2048], BF16)
        WRT = sb("wrtb", [128, 128], BF16)
        G = [sb(f"g{i}", [128, 512], F32) for i in range(4)]
        IDB = sb("idbs", [128, 128], BF16)
        IDF = sb("idfs", [128, 128], F32)
        ONESB = sb("onesbs", [128, 128], BF16)
        TM = {"k": sb("tmks", [128, DCn], F32), "r": sb("tmrs", [128, DCn], F32),
              "w": sb("tmws", [128, DCn], F32)}
        OMT = {m: sb(f"omt{m}", [128, DCn], F32) for m in "krw"}
        LNW = sb("lnws", [128, DCn], F32)
        LNB = sb("lnbs", [128, DCn], F32)
        TD = sb("tds", [128, HCn], F32)
        DD = sb("dds", [128, HCn], F32)
        MASK = sb("masks", [128, 1], F32)
        BRF = sb("brfs", [1, 8], F32)
        BRB = sb("brbs", [1, 8], BF16)
        PREV = sb("prevln", [128, DCn], BF16)
        CARRY = sb("carrys", [128, HCn], F32)
        SM = sb("smalls", [128, 96], F32)
        SOFT = sb("softs", [128, 320], F32)

        PA = psb("pa", [128, 1024])
        PB = psb("pb", [128, 1024])
        PC = psb("pc", [128, 1024])
        PD = psb("pd", [128, 1024])

        class Sem:
            def __init__(self, name):
                self.h = ctx.enter_context(nc.semaphore(name))
                self.n = 0
        ld = Sem("ld"); lx = Sem("lx"); pe = Sem("pe"); dv = Sem("dv")
        ac = Sem("ac"); st = Sem("st"); st2 = Sem("st2")
        LQ = [Sem("lq0"), Sem("lq1")]
        lxb = Sem("lxb")

        ORDER = []

        def emit(eng, meth, *args, wait=None, inc=None, **kwargs):
            wl = [(s.h, v) for (s, v) in (wait or [])]
            il = []
            for (s, k) in (inc or []):
                s.n += k
                il.append((s.h, k))
            def fn(E, meth=meth, args=args, kwargs=kwargs, wl=wl, il=il):
                for h, v in wl:
                    E.wait_ge(h, v)
                r = getattr(E, meth)(*args, **kwargs)
                for h, k in il:
                    r.then_inc(h, k)
            ORDER.append((eng, fn))

        def emit_wait(eng, wait):
            wl = [(s.h, v) for (s, v) in wait]
            def fn(E, wl=wl):
                for h, v in wl:
                    E.wait_ge(h, v)
            ORDER.append((eng, fn))

        S1b = ARENA[:, 0:8192].bitcast(BF16)
        S2b = ARENA[:, 8192:16384].bitcast(BF16)
        S3b = ARENA[:, 16384:24576].bitcast(BF16)
        ATTN = ARENA[:, 8192:24576]
        ATTNB = ARENA[:, 16384:24576].bitcast(BF16)

        def s1c(fc, ttok):
            return S1b[:, fc * ttok:(fc + 1) * ttok]
        OUTH = KREG[:, 0:7168].bitcast(BF16)
        def s1outh(hc):
            return OUTH[:, hc * TT:(hc + 1) * TT]
        def attn_c(fc):
            return ATTN[:, fc * TT:(fc + 1) * TT]
        def attnb_c(fc):
            return ATTNB[:, fc * TT:(fc + 1) * TT]
        def k2v(hc, ttok=TT):
            return KREG[:, hc * 1024: hc * 1024 + ttok]

        LN1B = KREG[:, 0:8192].bitcast(BF16)
        def ln1c(fc):
            return LN1B[:, fc * TT:(fc + 1) * TT]

        SROW_X = KREG[0:1, 8192:9216]
        SROW_Q = KREG[0:1, 9216:10240]
        R_MU = KREG[0:1, 10240:11264]
        R_E2 = KREG[0:1, 11264:12288]
        R_VE = KREG[0:1, 12288:13312]
        R_TN = KREG[0:1, 8192:9216]
        R_YN = KREG[0:1, 9216:10240]
        BROW_R = KREG[0:1, 13312:13824].bitcast(BF16)
        BROW_M = KREG[0:1, 13824:14336].bitcast(BF16)
        BCC_R = KREG[:, 11264:11776].bitcast(BF16)
        BCC_M = KREG[:, 11776:12288].bitcast(BF16)
        HPR = [KREG[:, 8192:8448].bitcast(BF16), KREG[:, 8448:8704].bitcast(BF16)]
        HT = KREG[:, 8704:9728].bitcast(BF16)
        BCA_R = KREG[:, 14336:14848].bitcast(BF16)
        BCA_M = KREG[:, 14848:15360].bitcast(BF16)
        WSB = [w.bitcast(BF16) for w in WS]

        guards = {}
        def gwait(name):
            return guards.pop(name, [])
        def gset(name, sem):
            guards.setdefault(name, []).append((sem, sem.n))

        for ap_sb, ap_d in [(IDB[:], idb_d), (IDF[:], idf_d), (ONESB[:], onesb_d),
                            (TM["k"][:], tmk_d), (TM["r"][:], tmr_d), (TM["w"][:], tmw_d),
                            (LNW[:], lnw_d), (LNB[:], lnb_d), (TD[:], td_d),
                            (MASK[:], mask_d), (BRF[:], brf_d)]:
            emit("sync", "dma_start", out=ap_sb, in_=ap_d, inc=[(lx, 16)])
        lx_consts = lx.n
        for ap_sb, ap_d in [(ACAT[:], acat_d), (BCAT[:], bcat_d), (WRT[:], wrt_d)]:
            emit("gpsimd", "dma_start", out=ap_sb, in_=ap_d, inc=[(ld, 16)])
        ld_ffn = ld.n

        emit("scalar", "activation", out=DD[:], in_=TD[:], func=AF.Exp,
             wait=[(lx, lx_consts)], inc=[(ac, 1)])
        emit("scalar", "activation", out=DD[:], in_=DD[:], func=AF.Exp, scale=-1.0,
             wait=[(ac, ac.n)], inc=[(ac, 1)])
        dd_ready = ac.n
        emit("scalar", "copy", out=BRB[:], in_=BRF[:], inc=[(ac, 1)])
        brb_ready = ac.n
        for m in "krw":
            emit("vector", "tensor_scalar", out=OMT[m][:], in0=TM[m][:],
                 scalar1=-1.0, scalar2=1.0, op0=ALU.mult, op1=ALU.add,
                 wait=[(lx, lx_consts)] if m == "k" else None,
                 inc=[(dv, 1)] if m == "w" else None)
        omt_ready = dv.n

        def ln_stats(ttok, get_bf16_chunk, stream_f32_chunk, bc_r, bc_m):
            nh = (ttok + 511) // 512
            pc_w = gwait("PC"); pd_w = gwait("PD")
            gfree = {0: gwait("G0"), 1: gwait("G1"),
                     2: gwait("G2"), 3: gwait("G3")}
            for fc in range(DCn):
                if get_bf16_chunk is not None:
                    src = get_bf16_chunk(fc)
                    cast_v = None
                else:
                    gi = fc % 2
                    emit("vector", "tensor_copy",
                         out=G[gi].bitcast(BF16)[:, 0:ttok],
                         in_=stream_f32_chunk(fc), wait=gfree[gi], inc=[(dv, 1)])
                    gfree[gi] = []
                    src = G[gi].bitcast(BF16)[:, 0:ttok]
                    cast_v = dv.n
                mmw = []
                if fc == 0:
                    mmw += pc_w
                if cast_v is not None:
                    mmw.append((dv, cast_v))
                for h in range(nh):
                    n0, n1 = h * 512, min((h + 1) * 512, ttok)
                    emit("tensor", "matmul", PC[0:1, n0:n1], ONESB[:, 0:1],
                         src[:, n0:n1], start=(fc == 0), stop=(fc == DCn - 1),
                         wait=mmw if h == 0 else None,
                         inc=[(pe, 1)] if h == nh - 1 else None)
                sigx_v = pe.n
                gi2 = 2 + fc % 2
                sqw = list(gfree[gi2])
                if cast_v is not None:
                    sqw.append((dv, cast_v))
                emit("scalar", "activation", out=G[gi2].bitcast(BF16)[:, 0:ttok],
                     in_=src, func=AF.Square, wait=sqw, inc=[(ac, 1)])
                gfree[gi2] = []
                sq_v = ac.n
                for h in range(nh):
                    n0, n1 = h * 512, min((h + 1) * 512, ttok)
                    mmw2 = [(ac, sq_v)] + (pd_w if fc == 0 else [])
                    emit("tensor", "matmul", PD[0:1, n0:n1], ONESB[:, 0:1],
                         G[gi2].bitcast(BF16)[:, n0:n1],
                         start=(fc == 0), stop=(fc == DCn - 1),
                         wait=mmw2 if h == 0 else None,
                         inc=[(pe, 1)] if h == nh - 1 else None)
                sigq_v = pe.n
                if cast_v is not None:
                    gfree[fc % 2] = [(pe, sigx_v), (ac, sq_v)]
                gfree[2 + fc % 2] = [(pe, sigq_v)]
            sig_done = pe.n
            for g, w in gfree.items():
                for s, v in w:
                    guards.setdefault(f"G{g}", []).append((s, v))
            emit("vector", "tensor_copy", out=SROW_X[0:1, 0:ttok],
                 in_=PC[0:1, 0:ttok], wait=[(pe, sig_done)] + gwait("SROWS"))
            emit("vector", "tensor_copy", out=SROW_Q[0:1, 0:ttok],
                 in_=PD[0:1, 0:ttok], inc=[(dv, 1)])
            gset("PC", dv); gset("PD", dv)
            mu = R_MU[0:1, 0:ttok]; e2 = R_E2[0:1, 0:ttok]
            ve_ = R_VE[0:1, 0:ttok]; tn = R_TN[0:1, 0:ttok]; yn = R_YN[0:1, 0:ttok]
            emit("vector", "tensor_scalar", out=mu, in0=SROW_X[0:1, 0:ttok],
                 scalar1=1.0 / D, scalar2=None, op0=ALU.mult)
            emit("vector", "tensor_scalar", out=e2, in0=SROW_Q[0:1, 0:ttok],
                 scalar1=1.0 / D, scalar2=EPS, op0=ALU.mult, op1=ALU.add)
            emit("vector", "tensor_tensor", out=ve_, in0=mu, in1=mu, op=ALU.mult)
            emit("vector", "scalar_tensor_tensor", out=ve_, in0=ve_, scalar=-1.0,
                 in1=e2, op0=ALU.mult, op1=ALU.add)
            emit("vector", "tensor_scalar", out=tn.bitcast(I32), in0=ve_.bitcast(I32),
                 scalar1=1, scalar2=None, op0=ALU.logical_shift_right)
            emit("vector", "tensor_copy", out=yn, in_=tn.bitcast(I32))
            emit("vector", "tensor_scalar", out=yn, in0=yn, scalar1=-1.0,
                 scalar2=MAGIC, op0=ALU.mult, op1=ALU.add)
            emit("vector", "tensor_copy", out=tn.bitcast(I32), in_=yn,
                 inc=[(dv, 1)])
            emit_wait("vector", [(dv, dv.n)])
            emit("vector", "tensor_tensor", out=yn, in0=tn, in1=tn, op=ALU.mult)
            emit("vector", "tensor_tensor", out=yn, in0=yn, in1=ve_, op=ALU.mult)
            emit("vector", "tensor_scalar", out=yn, in0=yn, scalar1=-0.5,
                 scalar2=1.5, op0=ALU.mult, op1=ALU.add)
            emit("vector", "tensor_tensor", out=yn, in0=yn, in1=tn, op=ALU.mult)
            for _ in range(2):
                emit("vector", "tensor_tensor", out=tn, in0=yn, in1=yn, op=ALU.mult)
                emit("vector", "tensor_tensor", out=tn, in0=tn, in1=ve_, op=ALU.mult)
                emit("vector", "tensor_scalar", out=tn, in0=tn, scalar1=-0.5,
                     scalar2=1.5, op0=ALU.mult, op1=ALU.add)
                emit("vector", "tensor_tensor", out=yn, in0=yn, in1=tn, op=ALU.mult)
            emit("vector", "tensor_copy", out=BROW_R[0:1, 0:ttok], in_=yn,
                 wait=gwait("BROWS"))
            emit("vector", "tensor_tensor", out=BROW_M[0:1, 0:ttok], in0=mu,
                 in1=yn, op=ALU.mult, inc=[(dv, 1)])
            brows_v = dv.n
            pc_w2 = gwait("PC"); pd_w2 = gwait("PD")
            for h in range(nh):
                n0, n1 = h * 512, min((h + 1) * 512, ttok)
                emit("tensor", "matmul", PC[:, n0:n1], ONESB[0:1, 0:128],
                     BROW_R[0:1, n0:n1], start=True, stop=True,
                     wait=([(dv, brows_v)] + pc_w2 + pd_w2) if h == 0 else None)
                emit("tensor", "matmul", PD[:, n0:n1], ONESB[0:1, 0:128],
                     BROW_M[0:1, n0:n1], start=True, stop=True,
                     inc=[(pe, 1)] if h == nh - 1 else None)
            bc_v = pe.n
            gset("BROWS", pe); gset("SROWS", pe)
            emit("scalar", "copy", out=bc_r[:, 0:ttok], in_=PC[:, 0:ttok],
                 wait=[(pe, bc_v)] + gwait("BCW"))
            emit("scalar", "copy", out=bc_m[:, 0:ttok], in_=PD[:, 0:ttok],
                 inc=[(ac, 1)])
            gset("PC", ac); gset("PD", ac)
            return ac.n

        def xb_load(ttok, xoff):
            s1_w = gwait("S1")
            emit("gpsimd", "dma_start",
                 out=S1b[:, 0:DCn * ttok].rearrange("p (fc t) -> p fc t", fc=DCn),
                 in_=xin[:, xoff:xoff + ttok].rearrange("(fc p) t -> p fc t", p=128),
                 wait=s1_w, inc=[(lxb, 16)])

        def a1_front(ttok, xoff, is_halo, load=True):
            if load:
                xb_load(ttok, xoff)
            xb_v = lxb.n
            emit_wait("tensor", [(lxb, xb_v), (lx, lx_consts)])
            emit_wait("scalar", [(lxb, xb_v)])
            bc_ready = ln_stats(ttok, get_bf16_chunk=lambda fc: s1c(fc, ttok),
                                stream_f32_chunk=None, bc_r=BCA_R, bc_m=BCA_M)
            for fc in range(DCn):
                emit("vector", "tensor_tensor", out=s1c(fc, ttok),
                     in0=s1c(fc, ttok), in1=BCA_R[:, 0:ttok], op=ALU.mult,
                     wait=[(ac, bc_ready)] if fc == 0 else None)
                emit("vector", "tensor_tensor", out=s1c(fc, ttok),
                     in0=s1c(fc, ttok), in1=BCA_M[:, 0:ttok], op=ALU.subtract,
                     inc=[(dv, 1)] if fc == DCn - 1 else None)
            gset("BCW", dv)

        wslot_guard = [[], []]

        def stream_weight(dram_ap, idx, slot, width):
            emit("gpsimd", "dma_start", out=WSB[slot][:, 0:width],
                 in_=dram_ap[idx, :, :], wait=wslot_guard[slot],
                 inc=[(LQ[slot], 16)])
            wslot_guard[slot] = []
            return (LQ[slot], LQ[slot].n)

        def a2_front(ttok, is_halo, premix_k=None):
            nh = (ttok + 511) // 512
            mixes = ["k"] if is_halo else ["k", "r", "w"]
            wdram = {"k": wkst, "r": wrst, "w": wgst}
            mixbuf = {"k": S2b, "r": S3b, "w": S2b}
            proj_done = {}
            mix_ready = {}
            evac_sq = {}
            scan_done = {}
            gate_guard = [[], []]

            def emit_mix(m, extra_guard):
                dst = mixbuf[m]
                w = list(extra_guard)
                for fc in range(DCn):
                    c = dst[:, fc * ttok:(fc + 1) * ttok]
                    emit("vector", "tensor_scalar", out=c, in0=s1c(fc, ttok),
                         scalar1=OMT[m][:, fc:fc + 1], scalar2=None, op0=ALU.mult,
                         wait=w if fc == 0 else None)
                    emit("vector", "scalar_tensor_tensor", out=c[:, 1:ttok],
                         in0=s1c(fc, ttok)[:, 0:ttok - 1],
                         scalar=TM[m][:, fc:fc + 1], in1=c[:, 1:ttok],
                         op0=ALU.mult, op1=ALU.add)
                    pcol = (s1c(fc, ttok)[:, 0:1] if is_halo
                            else PREV[:, fc:fc + 1])
                    emit("vector", "scalar_tensor_tensor", out=c[:, 0:1],
                         in0=pcol, scalar=TM[m][:, fc:fc + 1], in1=c[:, 0:1],
                         op0=ALU.mult, op1=ALU.add,
                         inc=[(dv, 1)] if fc == DCn - 1 else None)
                mix_ready[m] = dv.n

            if premix_k is not None:
                mix_ready["k"] = premix_k
            else:
                emit_mix("k", [(dv, omt_ready)] + gwait("S2"))
            if not is_halo:
                emit_mix("r", gwait("S3"))

            for m in mixes:
                for hc in range(HCn):
                    slot = hc % 2
                    lv = stream_weight(wdram[m], hc, slot, D)
                    pab = PA if hc % 2 == 0 else PB
                    nm = "PA" if hc % 2 == 0 else "PB"
                    bank_w = gwait(nm)
                    for fc in range(DCn):
                        for h in range(nh):
                            n0, n1 = h * 512, min((h + 1) * 512, ttok)
                            w = None
                            if h == 0 and fc == 0:
                                w = [lv, (dv, mix_ready[m])] + bank_w
                            emit("tensor", "matmul", pab[:, n0:n1],
                                 WSB[slot][:, fc * 128:(fc + 1) * 128],
                                 mixbuf[m][:, fc * ttok + n0: fc * ttok + n1],
                                 start=(fc == 0), stop=(fc == DCn - 1), wait=w,
                                 inc=[(pe, 1)] if fc == DCn - 1 else None)
                    proj_done[(m, hc)] = pe.n
                    wslot_guard[slot] = [(pe, pe.n)]
                    if m == "k":
                        for h in range(nh):
                            n0, n1 = h * 512, min((h + 1) * 512, ttok)
                            emit("scalar", "activation",
                                 out=k2v(hc, ttok)[:, n0:n1], in_=pab[:, n0:n1],
                                 func=AF.Relu,
                                 wait=[(pe, proj_done[(m, hc)] - nh + 1 + h)],
                                 inc=[(ac, 1)])
                        gset(nm, ac)
                        emit("scalar", "activation", out=k2v(hc, ttok),
                             in_=k2v(hc, ttok), func=AF.Square,
                             wait=[(ac, ac.n)], inc=[(ac, 1)])
                        evac_sq[hc] = ac.n
                    elif m == "r":
                        emit("vector", "tensor_tensor", out=k2v(hc, ttok),
                             in0=pab[:, 0:ttok], in1=k2v(hc, ttok), op=ALU.mult,
                             wait=[(pe, proj_done[(m, hc)]), (dv, scan_done[hc])],
                             inc=[(dv, 1)])
                        gset(nm, dv)
                    else:
                        for h in range(nh):
                            n0, n1 = h * 512, min((h + 1) * 512, ttok)
                            gb = G[h]
                            emit("scalar", "activation", out=gb[:, 0:n1 - n0],
                                 in_=pab[:, n0:n1], func=AF.Tanh, scale=0.5,
                                 wait=[(pe, proj_done[(m, hc)] - nh + 1 + h)]
                                      + gate_guard[h] + gwait(f"G{h}"),
                                 inc=[(ac, 1)])
                            gate_guard[h] = []
                            tanh_v = ac.n
                            emit("vector", "tensor_scalar", out=gb[:, 0:n1 - n0],
                                 in0=gb[:, 0:n1 - n0], scalar1=0.5, scalar2=0.5,
                                 op0=ALU.mult, op1=ALU.add, wait=[(ac, tanh_v)])
                            emit("vector", "tensor_tensor",
                                 out=s1outh(hc)[:, n0:n1],
                                 in0=k2v(hc, ttok)[:, n0:n1],
                                 in1=gb[:, 0:n1 - n0], op=ALU.mult,
                                 inc=[(dv, 1)])
                            gate_guard[h] = [(dv, dv.n)]
                        gset(nm, dv)
                if m == "k":
                    for hc in range(HCn):
                        init = 0.0 if is_halo else CARRY[:, hc:hc + 1]
                        emit("vector", "tensor_tensor_scan", out=k2v(hc, ttok),
                             data0=DD[:, hc:hc + 1].broadcast_to([128, ttok]),
                             data1=k2v(hc, ttok), initial=init,
                             op0=ALU.mult, op1=ALU.add,
                             wait=[(ac, evac_sq[hc]), (ac, dd_ready)],
                             inc=[(dv, 1)])
                        scan_done[hc] = dv.n
                    for hc in range(HCn):
                        if is_halo:
                            emit("vector", "tensor_scalar",
                                 out=CARRY[:, hc:hc + 1],
                                 in0=k2v(hc, ttok)[:, ttok - 1:ttok],
                                 scalar1=MASK[:, 0:1], scalar2=None, op0=ALU.mult)
                        else:
                            emit("vector", "tensor_copy",
                                 out=CARRY[:, hc:hc + 1],
                                 in_=k2v(hc, ttok)[:, ttok - 1:ttok])
                if m == "r":
                    emit_mix("w", [(pe, proj_done[("k", HCn - 1)])])
                    emit("vector", "tensor_copy", out=PREV[:, :],
                         in_=S1b[:, ttok - 1:DCn * ttok:ttok], inc=[(dv, 1)])
                    guards["S1"] = [(dv, dv.n)]
            if is_halo:
                emit("vector", "tensor_copy", out=PREV[:, :],
                     in_=S1b[:, ttok - 1:DCn * ttok:ttok],
                     wait=[(pe, proj_done[("k", HCn - 1)])], inc=[(dv, 1)])
                guards["S1"] = [(dv, dv.n)]
                guards["S2"] = [(pe, proj_done[("k", HCn - 1)])]
            return dv.n

        a1_front(HALO, 0, is_halo=True)
        a2_front(HALO, is_halo=True)
        a1_front(TT, HALO, is_halo=False)

        for tile in range(NTILES):
            xoff = HALO + tile * TT
            outh_all = a2_front(TT, is_halo=False)

            if tile + 1 < NTILES:
                xb_load(TT, HALO + (tile + 1) * TT)
            attn_evac = {}
            for dc in range(DCn):
                slot = dc % 2
                lv = stream_weight(wvst, dc, slot, H)
                pab = PA if dc % 2 == 0 else PB
                nm = "PA" if dc % 2 == 0 else "PB"
                bank_w = gwait(nm)
                for hc in range(HCn):
                    for h in range(2):
                        n0, n1 = h * 512, (h + 1) * 512
                        w = None
                        if h == 0 and hc == 0:
                            w = [lv, (dv, outh_all)] + bank_w
                        emit("tensor", "matmul", pab[:, n0:n1],
                             WSB[slot][:, hc * 128:(hc + 1) * 128],
                             s1outh(hc)[:, n0:n1],
                             start=(hc == 0), stop=(hc == HCn - 1), wait=w,
                             inc=[(pe, 1)] if hc == HCn - 1 else None)
                wv_v = pe.n
                wslot_guard[slot] = [(pe, pe.n)]
                emit("scalar", "copy", out=attnb_c(dc), in_=pab[:, 0:TT],
                     wait=[(pe, wv_v)] + (gwait("ATTN") if dc == 0 else []),
                     inc=[(ac, 1)])
                attn_evac[dc] = ac.n
                gset(nm, ac)

            if tile + 1 < NTILES:
                a1_front(TT, HALO + (tile + 1) * TT, is_halo=False, load=False)

            emit_wait("scalar", [(ac, max(attn_evac.values()))])
            emit_wait("tensor", [(ac, max(attn_evac.values()))])
            bcc_ready = ln_stats(TT, get_bf16_chunk=lambda fc: attnb_c(fc),
                                 stream_f32_chunk=None, bc_r=BCC_R, bc_m=BCC_M)
            x1_done = {}
            for fc in range(DCn):
                slot = fc % 2
                emit("sync", "dma_start", out=WS[slot][:, 0:TT],
                     in_=xin[fc * 128:(fc + 1) * 128, xoff:xoff + TT],
                     wait=wslot_guard[slot], inc=[(LQ[slot], 16)])
                wslot_guard[slot] = []
                xs_v = (LQ[slot], LQ[slot].n)
                gb = G[fc % 2].bitcast(BF16)[:, 0:TT]
                gw = gwait(f"G{fc % 2}")
                emit("vector", "tensor_tensor", out=gb, in0=attnb_c(fc),
                     in1=BCC_R[:, 0:TT], op=ALU.mult,
                     wait=([(ac, bcc_ready)] if fc == 0 else []) + gw)
                emit("vector", "tensor_tensor", out=gb, in0=gb,
                     in1=BCC_M[:, 0:TT], op=ALU.subtract)
                emit("vector", "tensor_scalar", out=gb, in0=gb,
                     scalar1=LNW[:, fc:fc + 1], scalar2=LNB[:, fc:fc + 1],
                     op0=ALU.mult, op1=ALU.add)
                emit("vector", "tensor_tensor", out=attn_c(fc), in0=gb,
                     in1=WS[slot][:, 0:TT], op=ALU.add,
                     wait=[xs_v], inc=[(dv, 1)])
                x1_done[fc] = dv.n
                wslot_guard[slot] = [(dv, dv.n)]
            gset("BCW", dv)
            gset("G0", dv); gset("G1", dv)

            bcc2_ready = ln_stats(TT, get_bf16_chunk=None,
                                  stream_f32_chunk=lambda fc: attn_c(fc),
                                  bc_r=BCC_R, bc_m=BCC_M)
            ln1b_done = {}
            for fc in range(DCn):
                emit("vector", "tensor_tensor", out=ln1c(fc), in0=attn_c(fc),
                     in1=BCC_R[:, 0:TT], op=ALU.mult,
                     wait=[(ac, bcc2_ready)] if fc == 0 else None)
                emit("vector", "tensor_tensor", out=ln1c(fc), in0=ln1c(fc),
                     in1=BCC_M[:, 0:TT], op=ALU.subtract, inc=[(dv, 1)])
                ln1b_done[fc] = dv.n
            ln1_v = dv.n
            gset("BCW", dv)

            tr_bank_guard = gwait("PA") + gwait("PB")
            bank_guard = {0: [], 1: [], 2: [], 3: []}
            pcw = gwait("PC"); pdw = gwait("PD")
            final_done = []
            hpr_guard = [[], []]
            ht_guard = []

            def ve(meth, *a, wait=None, **kw):
                emit("vector", meth, *a, wait=wait, inc=[(dv, 1)], **kw)
                emit_wait("vector", [(dv, dv.n)])

            L64 = SOFT[:, 0:64]
            sgreg = [PA[:, 0:8], PA[:, 512:520], PB[:, 0:8], PB[:, 512:520],
                     PC[:, 0:8], PC[:, 512:520], PD[:, 0:8], PD[:, 512:520]]
            sgbankw = [tr_bank_guard, [], [], [], pcw, [], pdw, []]
            for fc in range(DCn):
                for sg in range(8):
                    toff = sg * 128
                    w = list(sgbankw[sg]) if fc == 0 else None
                    if fc == 0 and sg == 0:
                        w += [(dv, ln1b_done[0]), (ac, brb_ready), (ld, ld_ffn)]
                    elif sg == 0:
                        w = [(dv, ln1b_done[fc])]
                    emit("tensor", "matmul", sgreg[sg],
                         LN1B[:, fc * TT + toff: fc * TT + toff + 128],
                         WRT[:, fc * 8:(fc + 1) * 8],
                         start=(fc == 0), stop=False, wait=w)
            for sg in range(8):
                emit("tensor", "matmul", sgreg[sg], ONESB[0:1, 0:128],
                     BRB[0:1, 0:8], start=False, stop=True,
                     inc=[(pe, 1)] if sg == 7 else None)
            lgv = pe.n
            for sg in range(8):
                emit("vector", "tensor_copy", out=L64[:, sg * 8:(sg + 1) * 8],
                     in_=sgreg[sg], wait=[(pe, lgv)] if sg == 0 else None,
                     inc=[(dv, 1)] if sg == 7 else None)
            lg_last = dv.n
            lg_evac = [(dv, lg_last)]
            pcw = lg_evac
            pdw = lg_evac
            tr_bank_guard = tr_bank_guard + lg_evac
            emit_wait("vector", [(dv, lg_last)])

            L3 = SOFT[:, 0:64].rearrange("p (s e) -> p s e", e=8)
            CMP = SOFT[:, 64:128]
            CMP3 = CMP.rearrange("p (s e) -> p s e", e=8)
            CA = SOFT[:, 128:192]
            CA3 = CA.rearrange("p (s e) -> p s e", e=8)
            CB = SOFT[:, 256:320]
            CB3 = CB.rearrange("p (s e) -> p s e", e=8)
            COEF = SOFT[:, 192:256]
            m4 = SM[:, 0:32]; m43 = m4.rearrange("p (s e) -> p s e", e=4)
            m2 = SM[:, 32:48]; m23 = m2.rearrange("p (s e) -> p s e", e=2)
            m8 = SM[:, 48:56]
            m8b = SM[:, 48:56].rearrange("p (s e) -> p s e", e=1).broadcast_to([128, 8, 8])
            s8 = SM[:, 56:64]
            rs8 = SM[:, 64:72]
            rt8 = SM[:, 72:80]
            w88 = SM[:, 80:88]
            w88b = SM[:, 80:88].rearrange("p (s e) -> p s e", e=1).broadcast_to([128, 8, 8])
            me8 = SM[:, 88:96]
            me8b = SM[:, 88:96].rearrange("p (s e) -> p s e", e=1).broadcast_to([128, 8, 8])
            ve("tensor_tensor", out=m43, in0=L3[:, :, 0:4], in1=L3[:, :, 4:8],
               op=ALU.max)
            ve("tensor_tensor", out=m23, in0=m43[:, :, 0:2], in1=m43[:, :, 2:4],
               op=ALU.max)
            ve("tensor_tensor", out=m8.rearrange("p (s e) -> p s e", e=1),
               in0=m23[:, :, 0:1], in1=m23[:, :, 1:2], op=ALU.max)
            ve("tensor_tensor", out=L3, in0=L3, in1=m8b, op=ALU.subtract)
            emit("scalar", "activation", out=L64, in_=L64, func=AF.Exp,
                 wait=[(dv, dv.n)], inc=[(ac, 1)])
            exp_v = ac.n
            ve("tensor_tensor", out=m43, in0=L3[:, :, 0:4], in1=L3[:, :, 4:8],
               op=ALU.add, wait=[(ac, exp_v)])
            ve("tensor_tensor", out=m23, in0=m43[:, :, 0:2], in1=m43[:, :, 2:4],
               op=ALU.add)
            ve("tensor_tensor", out=s8.rearrange("p (s e) -> p s e", e=1),
               in0=m23[:, :, 0:1], in1=m23[:, :, 1:2], op=ALU.add)
            ve("tensor_copy", out=rt8, in_=s8.bitcast(I32))
            ve("tensor_scalar", out=rt8, in0=rt8, scalar1=-1.0,
               scalar2=RMAGIC, op0=ALU.mult, op1=ALU.add)
            ve("tensor_copy", out=rs8.bitcast(I32), in_=rt8)
            for _i in range(2):
                ve("tensor_tensor", out=rt8, in0=s8, in1=rs8, op=ALU.mult)
                ve("tensor_scalar", out=rt8, in0=rt8, scalar1=-1.0, scalar2=2.0,
                   op0=ALU.mult, op1=ALU.add)
                ve("tensor_tensor", out=rs8, in0=rs8, in1=rt8, op=ALU.mult)
            ve("tensor_tensor", out=m43, in0=L3[:, :, 0:4], in1=L3[:, :, 4:8],
               op=ALU.max)
            ve("tensor_tensor", out=m23, in0=m43[:, :, 0:2], in1=m43[:, :, 2:4],
               op=ALU.max)
            ve("tensor_tensor", out=me8.rearrange("p (s e) -> p s e", e=1),
               in0=m23[:, :, 0:1], in1=m23[:, :, 1:2], op=ALU.max)
            ve("tensor_tensor", out=CMP3, in0=L3, in1=me8b, op=ALU.is_ge)
            ve("tensor_copy", out=CA, in_=CMP)
            ve("tensor_copy", out=CB, in_=CA)
            ve("tensor_tensor", out=CB3[:, :, 1:8], in0=CA3[:, :, 0:7],
               in1=CA3[:, :, 1:8], op=ALU.add)
            ve("tensor_copy", out=CA, in_=CB)
            ve("tensor_tensor", out=CA3[:, :, 2:8], in0=CB3[:, :, 0:6],
               in1=CB3[:, :, 2:8], op=ALU.add)
            ve("tensor_copy", out=CB, in_=CA)
            ve("tensor_tensor", out=CB3[:, :, 4:8], in0=CA3[:, :, 0:4],
               in1=CA3[:, :, 4:8], op=ALU.add)
            ve("tensor_scalar", out=CA, in0=CB, scalar1=-1.0, scalar2=2.0,
               op0=ALU.mult, op1=ALU.add)
            ve("tensor_scalar", out=CA, in0=CA, scalar1=1.0, scalar2=0.0,
               op0=ALU.min, op1=ALU.max)
            ve("tensor_tensor", out=CA, in0=CMP, in1=CA, op=ALU.mult)
            ve("tensor_scalar", out=w88, in0=rs8, scalar1=1.0 / 64.0,
               scalar2=None, op0=ALU.mult)
            ve("tensor_tensor", out=COEF.rearrange("p (s e) -> p s e", e=8),
               in0=CA3, in1=w88b, op=ALU.mult)
            route_v = dv.n

            for half in range(2):
                for s4 in range(4):
                    sg = half * 4 + s4
                    toff = sg * 128
                    for fc in range(DCn):
                        w = pdw if fc == 0 else None
                        pdw = []
                        emit("tensor", "matmul", PD[:, 0:512],
                             LN1B[:, fc * TT + toff: fc * TT + toff + 128],
                             ACAT[:, fc * 512:(fc + 1) * 512],
                             start=(fc == 0), stop=(fc == DCn - 1), wait=w,
                             inc=[(pe, 1)] if fc == DCn - 1 else None)
                    h_v = pe.n
                    hp = HPR[s4 % 2]
                    for e in range(8):
                        emit("vector", "tensor_scalar",
                             out=hp[:, e * 64:(e + 1) * 64],
                             in0=PD[:, e * 64:(e + 1) * 64],
                             scalar1=COEF[:, sg * 8 + e:sg * 8 + e + 1],
                             scalar2=None, op0=ALU.mult,
                             wait=([(pe, h_v), (dv, route_v)] + hpr_guard[s4 % 2])
                                  if e == 0 else None,
                             inc=[(dv, 1)] if e == 7 else None)
                        hpr_guard[s4 % 2] = []
                    hp_v = dv.n
                    pdw = [(dv, hp_v)]
                    trdst = [PA[:, 0:128], PA[:, 512:640],
                             PB[:, 0:128], PB[:, 512:640]]
                    for c in range(4):
                        w = None
                        if c == 0:
                            w = [(dv, hp_v)] + tr_bank_guard
                        emit("tensor", "matmul", trdst[c],
                             hp[:, c * 128:(c + 1) * 128], IDB[:],
                             start=True, stop=True, wait=w,
                             inc=[(pe, 1)] if c == 3 else None)
                    tr_v = pe.n
                    hpr_guard[s4 % 2] = [(pe, tr_v)]
                    for c in range(4):
                        emit("vector", "tensor_copy",
                             out=HT[:, c * 512 + s4 * 128: c * 512 + s4 * 128 + 128],
                             in_=trdst[c],
                             wait=([(pe, tr_v)] + (ht_guard if s4 == 0 else []))
                                  if c == 0 else None,
                             inc=[(dv, 1)] if c == 3 else None)
                    ht_guard = []
                    tr_bank_guard = [(dv, dv.n)]
                ht_v = dv.n
                banks = [(PA, 0, 0), (PA, 512, 1), (PB, 0, 2), (PB, 512, 3)]
                for dc in range(DCn):
                    pbk, boff, bid = banks[dc % 4]
                    bw = bank_guard.get(bid, []) + (tr_bank_guard if dc == 0 else [])
                    bank_guard[bid] = []
                    for er in range(4):
                        w = ([(dv, ht_v)] + bw) if er == 0 else None
                        emit("tensor", "matmul", pbk[:, boff:boff + 512],
                             BCAT[:, er * 2048 + dc * 128: er * 2048 + (dc + 1) * 128],
                             HT[:, er * 512:(er + 1) * 512],
                             start=(er == 0), stop=(er == 3), wait=w,
                             inc=[(pe, 1)] if er == 3 else None)
                    bv = pe.n
                    emit("vector", "tensor_tensor",
                         out=attn_c(dc)[:, half * 512:(half + 1) * 512],
                         in0=pbk[:, boff:boff + 512],
                         in1=attn_c(dc)[:, half * 512:(half + 1) * 512],
                         op=ALU.add, wait=[(pe, bv)], inc=[(dv, 1)])
                    final_done.append(dv.n)
                    bank_guard[bid] = [(dv, dv.n)]
                    ssem = st2 if dc < 8 else st
                    emit("sync" if half == 1 else "scalar", "dma_start",
                         out=out_d[:, tile * TT:(tile + 1) * TT].rearrange(
                             "(fc p) t -> p fc t", p=128)[:, dc:dc + 1,
                             half * 512:(half + 1) * 512],
                         in_=ATTN[:, dc * TT + half * 512:dc * TT
                                  + (half + 1) * 512].rearrange(
                             "p (fc t) -> p fc t", fc=1),
                         wait=[(dv, dv.n)], inc=[(ssem, 16)])
                ht_guard = [(pe, pe.n)]
                tr_bank_guard = (bank_guard[0] + bank_guard[1] + bank_guard[2]
                                 + bank_guard[3])
            guards["PA"] = bank_guard[0] + bank_guard[1]
            guards["PB"] = bank_guard[2] + bank_guard[3]
            guards["PC"] = pcw
            guards["PD"] = pdw

            guards["S2"] = [(st2, st2.n)]
            guards["ATTN"] = [(st, st.n), (st2, st2.n)]
            guards["S3"] = [(st, st.n)]

        emit_wait("sync", [(st, st.n), (st2, st2.n)])

        with nc.Block() as block:
            for eng in ["sync", "gpsimd", "vector", "scalar", "tensor"]:
                fns = [f for e, f in ORDER if e == eng]
                def runner(E, fns=fns):
                    for f in fns:
                        f(E)
                getattr(block, eng)(runner)
    return nc


def _stage_weights(inputs):
    f32 = np.float32
    Wk = np.asarray(inputs["L0_attn_Wk"], f32)
    Wr = np.asarray(inputs["L0_attn_Wr"], f32)
    Wg = np.asarray(inputs["L0_attn_Wg"], f32)
    Wv = np.asarray(inputs["L0_attn_Wv"], f32)
    A = np.asarray(inputs["L0_ffn_A"], f32)
    Bw = np.asarray(inputs["L0_ffn_B"], f32)
    WrF = np.asarray(inputs["L0_ffn_Wr"], f32)

    def projst(W):
        return np.ascontiguousarray(
            W.reshape(HCn, 128, DCn, 128).transpose(0, 3, 2, 1).reshape(HCn, 128, D))

    wvst = np.ascontiguousarray(
        Wv.reshape(DCn, 128, HCn, 128).transpose(0, 3, 2, 1).reshape(DCn, 128, H))
    acat = np.ascontiguousarray(
        A.transpose(2, 0, 1).reshape(DCn, 128, 512).transpose(1, 0, 2)
        .reshape(128, DCn * 512))
    Bp = Bw.transpose(0, 2, 1).reshape(512, 2048)
    bcat = np.ascontiguousarray(
        Bp.reshape(4, 128, 2048).transpose(1, 0, 2).reshape(128, 4 * 2048))
    wrt = np.ascontiguousarray(
        WrF.T.reshape(DCn, 128, 8).transpose(1, 0, 2).reshape(128, 128))

    def vec16(v):
        return np.ascontiguousarray(np.asarray(v, f32).reshape(DCn, 128).T)
    def vec14(v):
        return np.ascontiguousarray(np.asarray(v, f32).reshape(HCn, 128).T)

    return dict(
        wkst=projst(Wk), wrst=projst(Wr), wgst=projst(Wg), wvst=wvst,
        acat=acat, bcat=bcat, wrt=wrt,
        brf=np.ascontiguousarray(np.asarray(inputs["L0_ffn_br"], f32).reshape(1, 8)),
        tmk=vec16(inputs["L0_attn_time_mix_k"]),
        tmr=vec16(inputs["L0_attn_time_mix_r"]),
        tmw=vec16(inputs["L0_attn_time_mix_w"]),
        lnw=vec16(inputs["L0_attn_ln_w"]),
        lnb=vec16(inputs["L0_attn_ln_b"]),
        td=vec14(inputs["L0_attn_time_decay"]),
        idb=np.eye(128, dtype=ml_dtypes.bfloat16),
        idf=np.eye(128, dtype=np.float32),
        onesb=np.ones((128, 128), dtype=ml_dtypes.bfloat16),
    )


def kernel(**inputs):
    x = np.asarray(inputs["L0_attn_x"], np.float32)
    shared = _stage_weights(inputs)
    nc = build_program()

    in_maps = []
    for core in range(8):
        b, half = core // 2, core % 2
        t0 = half * NCORE
        xc = np.zeros((D, HALO + NCORE), np.float32)
        xc[:, HALO:] = x[b, t0:t0 + NCORE, :].T
        if half == 1:
            xc[:, 0:HALO] = x[b, t0 - HALO:t0, :].T
            mask = np.ones((128, 1), np.float32)
        else:
            xc[:, HALO - 1:HALO] = x[b, 1:2, :].T
            mask = np.zeros((128, 1), np.float32)
        m = dict(shared)
        m["xin"] = np.ascontiguousarray(xc)
        m["mask"] = mask
        in_maps.append(m)

    res = run_bass_kernel_spmd(nc, in_maps, core_ids=list(range(8)))
    out = np.empty((B, TFULL, D), np.float32)
    for core in range(8):
        b, half = core // 2, core % 2
        t0 = half * NCORE
        out[b, t0:t0 + NCORE, :] = res.results[core]["out"].T
    return out


# version 27
# speedup vs baseline: 1.0103x; 1.0103x over previous
import numpy as np
import ml_dtypes
from contextlib import ExitStack

import concourse.bass as bass
import concourse.mybir as mybir
from concourse.bass_utils import run_bass_kernel_spmd

F32 = mybir.dt.float32
BF16 = mybir.dt.bfloat16
I32 = mybir.dt.int32
ALU = mybir.AluOpType
AF = mybir.ActivationFunctionType
AX = mybir.AxisListType

B, TFULL, D, H = 4, 8192, 2048, 1792
NCORE = 4096
TT = 1024
NTILES = NCORE // TT
HALO = 256
DCn = D // 128
HCn = H // 128
EPS = 1e-5
MAGIC = 1597463007.0
RMAGIC = 2130520042.0


def build_program(debug=False):
    nc = bass.Bass("TRN2", target_bir_lowering=False)

    def din(name, shape, dt=F32):
        return nc.dram_tensor(name, list(shape), dt, kind="ExternalInput").ap()

    xin = din("xin", [D, HALO + NCORE])
    wkst = din("wkst", [HCn, 128, D])
    wrst = din("wrst", [HCn, 128, D])
    wgst = din("wgst", [HCn, 128, D])
    wvst = din("wvst", [DCn, 128, H])
    acat_d = din("acat", [128, DCn * 512])
    bcat_d = din("bcat", [128, 4 * 2048])
    wrt_d = din("wrt", [128, 128])
    brf_d = din("brf", [1, 8])
    tmk_d = din("tmk", [128, DCn]); tmr_d = din("tmr", [128, DCn]); tmw_d = din("tmw", [128, DCn])
    lnw_d = din("lnw", [128, DCn]); lnb_d = din("lnb", [128, DCn])
    td_d = din("td", [128, HCn])
    mask_d = din("mask", [128, 1])
    idb_d = din("idb", [128, 128], BF16)
    idf_d = din("idf", [128, 128])
    onesb_d = din("onesb", [128, 128], BF16)
    out_d = nc.dram_tensor("out", [D, NCORE], F32, kind="ExternalOutput").ap()

    with ExitStack() as ctx:
        sb = lambda name, shape, dt: ctx.enter_context(nc.sbuf_tensor(name, shape, dt))
        psb = lambda name, shape: ctx.enter_context(nc.psum_tensor(name, shape, F32))

        KREG = sb("kreg", [128, 15360], F32)
        ARENA = sb("arena", [128, 24576], F32)
        WS = [sb("ws0", [128, 1024], F32), sb("ws1", [128, 1024], F32)]
        ACAT = sb("acatb", [128, DCn * 512], BF16)
        BCAT = sb("bcatb", [128, 4 * 2048], BF16)
        WRT = sb("wrtb", [128, 128], BF16)
        G = [sb(f"g{i}", [128, 512], F32) for i in range(4)]
        IDB = sb("idbs", [128, 128], BF16)
        IDF = sb("idfs", [128, 128], F32)
        ONESB = sb("onesbs", [128, 128], BF16)
        TM = {"k": sb("tmks", [128, DCn], F32), "r": sb("tmrs", [128, DCn], F32),
              "w": sb("tmws", [128, DCn], F32)}
        OMT = {m: sb(f"omt{m}", [128, DCn], F32) for m in "krw"}
        LNW = sb("lnws", [128, DCn], F32)
        LNB = sb("lnbs", [128, DCn], F32)
        TD = sb("tds", [128, HCn], F32)
        DD = sb("dds", [128, HCn], F32)
        MASK = sb("masks", [128, 1], F32)
        BRF = sb("brfs", [1, 8], F32)
        BRB = sb("brbs", [1, 8], BF16)
        PREV = sb("prevln", [128, DCn], BF16)
        CARRY = sb("carrys", [128, HCn], F32)
        SM = sb("smalls", [128, 96], F32)
        SOFT = sb("softs", [128, 320], F32)

        PA = psb("pa", [128, 1024])
        PB = psb("pb", [128, 1024])
        PC = psb("pc", [128, 1024])
        PD = psb("pd", [128, 1024])

        class Sem:
            def __init__(self, name):
                self.h = ctx.enter_context(nc.semaphore(name))
                self.n = 0
        ld = Sem("ld"); lx = Sem("lx"); pe = Sem("pe"); dv = Sem("dv")
        ac = Sem("ac"); st = Sem("st"); st2 = Sem("st2")
        LQ = [Sem("lq0"), Sem("lq1")]
        lxb = Sem("lxb")

        ORDER = []

        def emit(eng, meth, *args, wait=None, inc=None, **kwargs):
            wl = [(s.h, v) for (s, v) in (wait or [])]
            il = []
            for (s, k) in (inc or []):
                s.n += k
                il.append((s.h, k))
            def fn(E, meth=meth, args=args, kwargs=kwargs, wl=wl, il=il):
                for h, v in wl:
                    E.wait_ge(h, v)
                r = getattr(E, meth)(*args, **kwargs)
                for h, k in il:
                    r.then_inc(h, k)
            ORDER.append((eng, fn))

        def emit_wait(eng, wait):
            wl = [(s.h, v) for (s, v) in wait]
            def fn(E, wl=wl):
                for h, v in wl:
                    E.wait_ge(h, v)
            ORDER.append((eng, fn))

        S1b = ARENA[:, 0:8192].bitcast(BF16)
        S2b = ARENA[:, 8192:16384].bitcast(BF16)
        S3b = ARENA[:, 16384:24576].bitcast(BF16)
        ATTN = ARENA[:, 8192:24576]
        ATTNB = ARENA[:, 16384:24576].bitcast(BF16)

        def s1c(fc, ttok):
            return S1b[:, fc * ttok:(fc + 1) * ttok]
        OUTH = KREG[:, 0:7168].bitcast(BF16)
        def s1outh(hc):
            return OUTH[:, hc * TT:(hc + 1) * TT]
        def attn_c(fc):
            return ATTN[:, fc * TT:(fc + 1) * TT]
        def attnb_c(fc):
            return ATTNB[:, fc * TT:(fc + 1) * TT]
        def k2v(hc, ttok=TT):
            return KREG[:, hc * 1024: hc * 1024 + ttok]

        LN1B = KREG[:, 0:8192].bitcast(BF16)
        def ln1c(fc):
            return LN1B[:, fc * TT:(fc + 1) * TT]

        SROW_X = KREG[0:1, 8192:9216]
        SROW_Q = KREG[0:1, 9216:10240]
        R_MU = KREG[0:1, 10240:11264]
        R_E2 = KREG[0:1, 11264:12288]
        R_VE = KREG[0:1, 12288:13312]
        R_TN = KREG[0:1, 8192:9216]
        R_YN = KREG[0:1, 9216:10240]
        BROW_R = KREG[0:1, 13312:13824].bitcast(BF16)
        BROW_M = KREG[0:1, 13824:14336].bitcast(BF16)
        BCC_R = KREG[:, 11264:11776].bitcast(BF16)
        BCC_M = KREG[:, 11776:12288].bitcast(BF16)
        HPR = [KREG[:, 8192:8448].bitcast(BF16), KREG[:, 8448:8704].bitcast(BF16)]
        HT = KREG[:, 8704:9728].bitcast(BF16)
        BCA_R = KREG[:, 14336:14848].bitcast(BF16)
        BCA_M = KREG[:, 14848:15360].bitcast(BF16)
        WSB = [w.bitcast(BF16) for w in WS]

        guards = {}
        def gwait(name):
            return guards.pop(name, [])
        def gset(name, sem):
            guards.setdefault(name, []).append((sem, sem.n))

        for ap_sb, ap_d in [(IDB[:], idb_d), (IDF[:], idf_d), (ONESB[:], onesb_d),
                            (TM["k"][:], tmk_d), (TM["r"][:], tmr_d), (TM["w"][:], tmw_d),
                            (LNW[:], lnw_d), (LNB[:], lnb_d), (TD[:], td_d),
                            (MASK[:], mask_d), (BRF[:], brf_d)]:
            emit("sync", "dma_start", out=ap_sb, in_=ap_d, inc=[(lx, 16)])
        lx_consts = lx.n
        for ap_sb, ap_d in [(ACAT[:], acat_d), (BCAT[:], bcat_d), (WRT[:], wrt_d)]:
            emit("gpsimd", "dma_start", out=ap_sb, in_=ap_d, inc=[(ld, 16)])
        ld_ffn = ld.n

        emit("scalar", "activation", out=DD[:], in_=TD[:], func=AF.Exp,
             wait=[(lx, lx_consts)], inc=[(ac, 1)])
        emit("scalar", "activation", out=DD[:], in_=DD[:], func=AF.Exp, scale=-1.0,
             wait=[(ac, ac.n)], inc=[(ac, 1)])
        dd_ready = ac.n
        emit("scalar", "copy", out=BRB[:], in_=BRF[:], inc=[(ac, 1)])
        brb_ready = ac.n
        for m in "krw":
            emit("vector", "tensor_scalar", out=OMT[m][:], in0=TM[m][:],
                 scalar1=-1.0, scalar2=1.0, op0=ALU.mult, op1=ALU.add,
                 wait=[(lx, lx_consts)] if m == "k" else None,
                 inc=[(dv, 1)] if m == "w" else None)
        omt_ready = dv.n

        def ln_stats(ttok, get_bf16_chunk, stream_f32_chunk, bc_r, bc_m):
            nh = (ttok + 511) // 512
            pc_w = gwait("PC"); pd_w = gwait("PD")
            gfree = {0: gwait("G0"), 1: gwait("G1"),
                     2: gwait("G2"), 3: gwait("G3")}
            for fc in range(DCn):
                if get_bf16_chunk is not None:
                    src = get_bf16_chunk(fc)
                    cast_v = None
                else:
                    gi = fc % 2
                    emit("vector", "tensor_copy",
                         out=G[gi].bitcast(BF16)[:, 0:ttok],
                         in_=stream_f32_chunk(fc), wait=gfree[gi], inc=[(dv, 1)])
                    gfree[gi] = []
                    src = G[gi].bitcast(BF16)[:, 0:ttok]
                    cast_v = dv.n
                mmw = []
                if fc == 0:
                    mmw += pc_w
                if cast_v is not None:
                    mmw.append((dv, cast_v))
                for h in range(nh):
                    n0, n1 = h * 512, min((h + 1) * 512, ttok)
                    emit("tensor", "matmul", PC[0:1, n0:n1], ONESB[:, 0:1],
                         src[:, n0:n1], start=(fc == 0), stop=(fc == DCn - 1),
                         wait=mmw if h == 0 else None,
                         inc=[(pe, 1)] if h == nh - 1 else None)
                sigx_v = pe.n
                gi2 = 2 + fc % 2
                sqw = list(gfree[gi2])
                if cast_v is not None:
                    sqw.append((dv, cast_v))
                emit("scalar", "activation", out=G[gi2].bitcast(BF16)[:, 0:ttok],
                     in_=src, func=AF.Square, wait=sqw, inc=[(ac, 1)])
                gfree[gi2] = []
                sq_v = ac.n
                for h in range(nh):
                    n0, n1 = h * 512, min((h + 1) * 512, ttok)
                    mmw2 = [(ac, sq_v)] + (pd_w if fc == 0 else [])
                    emit("tensor", "matmul", PD[0:1, n0:n1], ONESB[:, 0:1],
                         G[gi2].bitcast(BF16)[:, n0:n1],
                         start=(fc == 0), stop=(fc == DCn - 1),
                         wait=mmw2 if h == 0 else None,
                         inc=[(pe, 1)] if h == nh - 1 else None)
                sigq_v = pe.n
                if cast_v is not None:
                    gfree[fc % 2] = [(pe, sigx_v), (ac, sq_v)]
                gfree[2 + fc % 2] = [(pe, sigq_v)]
            sig_done = pe.n
            for g, w in gfree.items():
                for s, v in w:
                    guards.setdefault(f"G{g}", []).append((s, v))
            emit("vector", "tensor_copy", out=SROW_X[0:1, 0:ttok],
                 in_=PC[0:1, 0:ttok], wait=[(pe, sig_done)] + gwait("SROWS"))
            emit("vector", "tensor_copy", out=SROW_Q[0:1, 0:ttok],
                 in_=PD[0:1, 0:ttok], inc=[(dv, 1)])
            gset("PC", dv); gset("PD", dv)
            mu = R_MU[0:1, 0:ttok]; e2 = R_E2[0:1, 0:ttok]
            ve_ = R_VE[0:1, 0:ttok]; tn = R_TN[0:1, 0:ttok]; yn = R_YN[0:1, 0:ttok]
            emit("vector", "tensor_scalar", out=mu, in0=SROW_X[0:1, 0:ttok],
                 scalar1=1.0 / D, scalar2=None, op0=ALU.mult)
            emit("vector", "tensor_scalar", out=e2, in0=SROW_Q[0:1, 0:ttok],
                 scalar1=1.0 / D, scalar2=EPS, op0=ALU.mult, op1=ALU.add)
            emit("vector", "tensor_tensor", out=ve_, in0=mu, in1=mu, op=ALU.mult)
            emit("vector", "scalar_tensor_tensor", out=ve_, in0=ve_, scalar=-1.0,
                 in1=e2, op0=ALU.mult, op1=ALU.add)
            emit("vector", "tensor_scalar", out=tn.bitcast(I32), in0=ve_.bitcast(I32),
                 scalar1=1, scalar2=None, op0=ALU.logical_shift_right)
            emit("vector", "tensor_copy", out=yn, in_=tn.bitcast(I32))
            emit("vector", "tensor_scalar", out=yn, in0=yn, scalar1=-1.0,
                 scalar2=MAGIC, op0=ALU.mult, op1=ALU.add)
            emit("vector", "tensor_copy", out=tn.bitcast(I32), in_=yn,
                 inc=[(dv, 1)])
            emit_wait("vector", [(dv, dv.n)])
            emit("vector", "tensor_tensor", out=yn, in0=tn, in1=tn, op=ALU.mult)
            emit("vector", "tensor_tensor", out=yn, in0=yn, in1=ve_, op=ALU.mult)
            emit("vector", "tensor_scalar", out=yn, in0=yn, scalar1=-0.5,
                 scalar2=1.5, op0=ALU.mult, op1=ALU.add)
            emit("vector", "tensor_tensor", out=yn, in0=yn, in1=tn, op=ALU.mult)
            for it in range(2):
                emit("vector", "tensor_tensor", out=tn, in0=yn, in1=yn, op=ALU.mult)
                emit("vector", "tensor_tensor", out=tn, in0=tn, in1=ve_, op=ALU.mult)
                emit("vector", "tensor_scalar", out=tn, in0=tn, scalar1=-0.5,
                     scalar2=1.5, op0=ALU.mult, op1=ALU.add)
                if it == 0:
                    emit("vector", "tensor_tensor", out=yn, in0=yn, in1=tn,
                         op=ALU.mult)
                else:
                    emit("vector", "tensor_tensor", out=BROW_R[0:1, 0:ttok],
                         in0=yn, in1=tn, op=ALU.mult, wait=gwait("BROWS"))
            emit("vector", "tensor_tensor", out=BROW_M[0:1, 0:ttok], in0=mu,
                 in1=BROW_R[0:1, 0:ttok], op=ALU.mult, inc=[(dv, 1)])
            brows_v = dv.n
            pc_w2 = gwait("PC"); pd_w2 = gwait("PD")
            for h in range(nh):
                n0, n1 = h * 512, min((h + 1) * 512, ttok)
                emit("tensor", "matmul", PC[:, n0:n1], ONESB[0:1, 0:128],
                     BROW_R[0:1, n0:n1], start=True, stop=True,
                     wait=([(dv, brows_v)] + pc_w2 + pd_w2) if h == 0 else None)
                emit("tensor", "matmul", PD[:, n0:n1], ONESB[0:1, 0:128],
                     BROW_M[0:1, n0:n1], start=True, stop=True,
                     inc=[(pe, 1)] if h == nh - 1 else None)
            bc_v = pe.n
            gset("BROWS", pe); gset("SROWS", pe)
            emit("scalar", "copy", out=bc_r[:, 0:ttok], in_=PC[:, 0:ttok],
                 wait=[(pe, bc_v)] + gwait("BCW"))
            emit("scalar", "copy", out=bc_m[:, 0:ttok], in_=PD[:, 0:ttok],
                 inc=[(ac, 1)])
            gset("PC", ac); gset("PD", ac)
            return ac.n

        def xb_load(ttok, xoff):
            s1_w = gwait("S1")
            emit("gpsimd", "dma_start",
                 out=S1b[:, 0:DCn * ttok].rearrange("p (fc t) -> p fc t", fc=DCn),
                 in_=xin[:, xoff:xoff + ttok].rearrange("(fc p) t -> p fc t", p=128),
                 wait=s1_w, inc=[(lxb, 16)])

        def a1_front(ttok, xoff, is_halo, load=True):
            if load:
                xb_load(ttok, xoff)
            xb_v = lxb.n
            emit_wait("tensor", [(lxb, xb_v), (lx, lx_consts)])
            emit_wait("scalar", [(lxb, xb_v)])
            bc_ready = ln_stats(ttok, get_bf16_chunk=lambda fc: s1c(fc, ttok),
                                stream_f32_chunk=None, bc_r=BCA_R, bc_m=BCA_M)
            for fc in range(DCn):
                emit("vector", "tensor_tensor", out=s1c(fc, ttok),
                     in0=s1c(fc, ttok), in1=BCA_R[:, 0:ttok], op=ALU.mult,
                     wait=[(ac, bc_ready)] if fc == 0 else None)
                emit("vector", "tensor_tensor", out=s1c(fc, ttok),
                     in0=s1c(fc, ttok), in1=BCA_M[:, 0:ttok], op=ALU.subtract,
                     inc=[(dv, 1)] if fc == DCn - 1 else None)
            gset("BCW", dv)

        wslot_guard = [[], []]

        def stream_weight(dram_ap, idx, slot, width):
            emit("gpsimd", "dma_start", out=WSB[slot][:, 0:width],
                 in_=dram_ap[idx, :, :], wait=wslot_guard[slot],
                 inc=[(LQ[slot], 16)])
            wslot_guard[slot] = []
            return (LQ[slot], LQ[slot].n)

        def a2_front(ttok, is_halo, premix_k=None):
            nh = (ttok + 511) // 512
            mixes = ["k"] if is_halo else ["k", "r", "w"]
            wdram = {"k": wkst, "r": wrst, "w": wgst}
            mixbuf = {"k": S2b, "r": S3b, "w": S2b}
            proj_done = {}
            mix_ready = {}
            evac_sq = {}
            scan_done = {}
            gate_guard = [[], []]

            def emit_mix(m, extra_guard):
                dst = mixbuf[m]
                w = list(extra_guard)
                for fc in range(DCn):
                    c = dst[:, fc * ttok:(fc + 1) * ttok]
                    emit("vector", "tensor_scalar", out=c, in0=s1c(fc, ttok),
                         scalar1=OMT[m][:, fc:fc + 1], scalar2=None, op0=ALU.mult,
                         wait=w if fc == 0 else None)
                    emit("vector", "scalar_tensor_tensor", out=c[:, 1:ttok],
                         in0=s1c(fc, ttok)[:, 0:ttok - 1],
                         scalar=TM[m][:, fc:fc + 1], in1=c[:, 1:ttok],
                         op0=ALU.mult, op1=ALU.add)
                    pcol = (s1c(fc, ttok)[:, 0:1] if is_halo
                            else PREV[:, fc:fc + 1])
                    emit("vector", "scalar_tensor_tensor", out=c[:, 0:1],
                         in0=pcol, scalar=TM[m][:, fc:fc + 1], in1=c[:, 0:1],
                         op0=ALU.mult, op1=ALU.add,
                         inc=[(dv, 1)] if fc == DCn - 1 else None)
                mix_ready[m] = dv.n

            if premix_k is not None:
                mix_ready["k"] = premix_k
            else:
                emit_mix("k", [(dv, omt_ready)] + gwait("S2"))
            if not is_halo:
                emit_mix("r", gwait("S3"))

            for m in mixes:
                for hc in range(HCn):
                    slot = hc % 2
                    lv = stream_weight(wdram[m], hc, slot, D)
                    pab = PA if hc % 2 == 0 else PB
                    nm = "PA" if hc % 2 == 0 else "PB"
                    bank_w = gwait(nm)
                    for fc in range(DCn):
                        for h in range(nh):
                            n0, n1 = h * 512, min((h + 1) * 512, ttok)
                            w = None
                            if h == 0 and fc == 0:
                                w = [lv, (dv, mix_ready[m])] + bank_w
                            emit("tensor", "matmul", pab[:, n0:n1],
                                 WSB[slot][:, fc * 128:(fc + 1) * 128],
                                 mixbuf[m][:, fc * ttok + n0: fc * ttok + n1],
                                 start=(fc == 0), stop=(fc == DCn - 1), wait=w,
                                 inc=[(pe, 1)] if fc == DCn - 1 else None)
                    proj_done[(m, hc)] = pe.n
                    wslot_guard[slot] = [(pe, pe.n)]
                    if m == "k":
                        for h in range(nh):
                            n0, n1 = h * 512, min((h + 1) * 512, ttok)
                            emit("scalar", "activation",
                                 out=k2v(hc, ttok)[:, n0:n1], in_=pab[:, n0:n1],
                                 func=AF.Relu,
                                 wait=[(pe, proj_done[(m, hc)] - nh + 1 + h)],
                                 inc=[(ac, 1)])
                        gset(nm, ac)
                        emit("scalar", "activation", out=k2v(hc, ttok),
                             in_=k2v(hc, ttok), func=AF.Square,
                             wait=[(ac, ac.n)], inc=[(ac, 1)])
                        evac_sq[hc] = ac.n
                    elif m == "r":
                        emit("vector", "tensor_tensor", out=k2v(hc, ttok),
                             in0=pab[:, 0:ttok], in1=k2v(hc, ttok), op=ALU.mult,
                             wait=[(pe, proj_done[(m, hc)]), (dv, scan_done[hc])],
                             inc=[(dv, 1)])
                        gset(nm, dv)
                    else:
                        for h in range(nh):
                            n0, n1 = h * 512, min((h + 1) * 512, ttok)
                            gb = G[h]
                            emit("scalar", "activation", out=gb[:, 0:n1 - n0],
                                 in_=pab[:, n0:n1], func=AF.Tanh, scale=0.5,
                                 wait=[(pe, proj_done[(m, hc)] - nh + 1 + h)]
                                      + gate_guard[h] + gwait(f"G{h}"),
                                 inc=[(ac, 1)])
                            gate_guard[h] = []
                            tanh_v = ac.n
                            emit("vector", "tensor_scalar", out=gb[:, 0:n1 - n0],
                                 in0=gb[:, 0:n1 - n0], scalar1=0.5, scalar2=0.5,
                                 op0=ALU.mult, op1=ALU.add, wait=[(ac, tanh_v)])
                            emit("vector", "tensor_tensor",
                                 out=s1outh(hc)[:, n0:n1],
                                 in0=k2v(hc, ttok)[:, n0:n1],
                                 in1=gb[:, 0:n1 - n0], op=ALU.mult,
                                 inc=[(dv, 1)])
                            gate_guard[h] = [(dv, dv.n)]
                        gset(nm, dv)
                if m == "k":
                    for hc in range(HCn):
                        init = 0.0 if is_halo else CARRY[:, hc:hc + 1]
                        emit("vector", "tensor_tensor_scan", out=k2v(hc, ttok),
                             data0=DD[:, hc:hc + 1].broadcast_to([128, ttok]),
                             data1=k2v(hc, ttok), initial=init,
                             op0=ALU.mult, op1=ALU.add,
                             wait=[(ac, evac_sq[hc]), (ac, dd_ready)],
                             inc=[(dv, 1)])
                        scan_done[hc] = dv.n
                    for hc in range(HCn):
                        if is_halo:
                            emit("vector", "tensor_scalar",
                                 out=CARRY[:, hc:hc + 1],
                                 in0=k2v(hc, ttok)[:, ttok - 1:ttok],
                                 scalar1=MASK[:, 0:1], scalar2=None, op0=ALU.mult)
                        else:
                            emit("vector", "tensor_copy",
                                 out=CARRY[:, hc:hc + 1],
                                 in_=k2v(hc, ttok)[:, ttok - 1:ttok])
                if m == "r":
                    emit_mix("w", [(pe, proj_done[("k", HCn - 1)])])
                    emit("vector", "tensor_copy", out=PREV[:, :],
                         in_=S1b[:, ttok - 1:DCn * ttok:ttok], inc=[(dv, 1)])
                    guards["S1"] = [(dv, dv.n)]
            if is_halo:
                emit("vector", "tensor_copy", out=PREV[:, :],
                     in_=S1b[:, ttok - 1:DCn * ttok:ttok],
                     wait=[(pe, proj_done[("k", HCn - 1)])], inc=[(dv, 1)])
                guards["S1"] = [(dv, dv.n)]
                guards["S2"] = [(pe, proj_done[("k", HCn - 1)])]
            return dv.n

        a1_front(HALO, 0, is_halo=True)
        a2_front(HALO, is_halo=True)
        a1_front(TT, HALO, is_halo=False)

        for tile in range(NTILES):
            xoff = HALO + tile * TT
            outh_all = a2_front(TT, is_halo=False)

            if tile + 1 < NTILES:
                xb_load(TT, HALO + (tile + 1) * TT)
            attn_evac = {}
            for dc in range(DCn):
                slot = dc % 2
                lv = stream_weight(wvst, dc, slot, H)
                pab = PA if dc % 2 == 0 else PB
                nm = "PA" if dc % 2 == 0 else "PB"
                bank_w = gwait(nm)
                for hc in range(HCn):
                    for h in range(2):
                        n0, n1 = h * 512, (h + 1) * 512
                        w = None
                        if h == 0 and hc == 0:
                            w = [lv, (dv, outh_all)] + bank_w
                        emit("tensor", "matmul", pab[:, n0:n1],
                             WSB[slot][:, hc * 128:(hc + 1) * 128],
                             s1outh(hc)[:, n0:n1],
                             start=(hc == 0), stop=(hc == HCn - 1), wait=w,
                             inc=[(pe, 1)] if hc == HCn - 1 else None)
                wv_v = pe.n
                wslot_guard[slot] = [(pe, pe.n)]
                emit("scalar", "copy", out=attnb_c(dc), in_=pab[:, 0:TT],
                     wait=[(pe, wv_v)] + (gwait("ATTN") if dc == 0 else []),
                     inc=[(ac, 1)])
                attn_evac[dc] = ac.n
                gset(nm, ac)

            if tile + 1 < NTILES:
                a1_front(TT, HALO + (tile + 1) * TT, is_halo=False, load=False)

            emit_wait("scalar", [(ac, max(attn_evac.values()))])
            emit_wait("tensor", [(ac, max(attn_evac.values()))])
            bcc_ready = ln_stats(TT, get_bf16_chunk=lambda fc: attnb_c(fc),
                                 stream_f32_chunk=None, bc_r=BCC_R, bc_m=BCC_M)
            x1_done = {}
            for fc in range(DCn):
                slot = fc % 2
                emit("sync", "dma_start", out=WS[slot][:, 0:TT],
                     in_=xin[fc * 128:(fc + 1) * 128, xoff:xoff + TT],
                     wait=wslot_guard[slot], inc=[(LQ[slot], 16)])
                wslot_guard[slot] = []
                xs_v = (LQ[slot], LQ[slot].n)
                gb = G[fc % 2].bitcast(BF16)[:, 0:TT]
                gw = gwait(f"G{fc % 2}")
                emit("vector", "tensor_tensor", out=gb, in0=attnb_c(fc),
                     in1=BCC_R[:, 0:TT], op=ALU.mult,
                     wait=([(ac, bcc_ready)] if fc == 0 else []) + gw)
                emit("vector", "tensor_tensor", out=gb, in0=gb,
                     in1=BCC_M[:, 0:TT], op=ALU.subtract)
                emit("vector", "tensor_scalar", out=gb, in0=gb,
                     scalar1=LNW[:, fc:fc + 1], scalar2=LNB[:, fc:fc + 1],
                     op0=ALU.mult, op1=ALU.add)
                emit("vector", "tensor_tensor", out=attn_c(fc), in0=gb,
                     in1=WS[slot][:, 0:TT], op=ALU.add,
                     wait=[xs_v], inc=[(dv, 1)])
                x1_done[fc] = dv.n
                wslot_guard[slot] = [(dv, dv.n)]
            gset("BCW", dv)
            gset("G0", dv); gset("G1", dv)

            bcc2_ready = ln_stats(TT, get_bf16_chunk=None,
                                  stream_f32_chunk=lambda fc: attn_c(fc),
                                  bc_r=BCC_R, bc_m=BCC_M)
            ln1b_done = {}
            for fc in range(DCn):
                emit("vector", "tensor_tensor", out=ln1c(fc), in0=attn_c(fc),
                     in1=BCC_R[:, 0:TT], op=ALU.mult,
                     wait=[(ac, bcc2_ready)] if fc == 0 else None)
                emit("vector", "tensor_tensor", out=ln1c(fc), in0=ln1c(fc),
                     in1=BCC_M[:, 0:TT], op=ALU.subtract, inc=[(dv, 1)])
                ln1b_done[fc] = dv.n
            ln1_v = dv.n
            gset("BCW", dv)

            tr_bank_guard = gwait("PA") + gwait("PB")
            bank_guard = {0: [], 1: [], 2: [], 3: []}
            pcw = gwait("PC"); pdw = gwait("PD")
            final_done = []
            hpr_guard = [[], []]
            ht_guard = []

            def ve(meth, *a, wait=None, **kw):
                emit("vector", meth, *a, wait=wait, inc=[(dv, 1)], **kw)
                emit_wait("vector", [(dv, dv.n)])

            L64 = SOFT[:, 0:64]
            sgreg = [PA[:, 0:8], PA[:, 512:520], PB[:, 0:8], PB[:, 512:520],
                     PC[:, 0:8], PC[:, 512:520], PD[:, 0:8], PD[:, 512:520]]
            sgbankw = [tr_bank_guard, [], [], [], pcw, [], pdw, []]
            for fc in range(DCn):
                for sg in range(8):
                    toff = sg * 128
                    w = list(sgbankw[sg]) if fc == 0 else None
                    if fc == 0 and sg == 0:
                        w += [(dv, ln1b_done[0]), (ac, brb_ready), (ld, ld_ffn)]
                    elif sg == 0:
                        w = [(dv, ln1b_done[fc])]
                    emit("tensor", "matmul", sgreg[sg],
                         LN1B[:, fc * TT + toff: fc * TT + toff + 128],
                         WRT[:, fc * 8:(fc + 1) * 8],
                         start=(fc == 0), stop=False, wait=w)
            for sg in range(8):
                emit("tensor", "matmul", sgreg[sg], ONESB[0:1, 0:128],
                     BRB[0:1, 0:8], start=False, stop=True,
                     inc=[(pe, 1)] if sg == 7 else None)
            lgv = pe.n
            for sg in range(8):
                emit("vector", "tensor_copy", out=L64[:, sg * 8:(sg + 1) * 8],
                     in_=sgreg[sg], wait=[(pe, lgv)] if sg == 0 else None,
                     inc=[(dv, 1)] if sg == 7 else None)
            lg_last = dv.n
            lg_evac = [(dv, lg_last)]
            pcw = lg_evac
            pdw = lg_evac
            tr_bank_guard = tr_bank_guard + lg_evac
            emit_wait("vector", [(dv, lg_last)])

            L3 = SOFT[:, 0:64].rearrange("p (s e) -> p s e", e=8)
            CMP = SOFT[:, 64:128]
            CMP3 = CMP.rearrange("p (s e) -> p s e", e=8)
            CA = SOFT[:, 128:192]
            CA3 = CA.rearrange("p (s e) -> p s e", e=8)
            CB = SOFT[:, 256:320]
            CB3 = CB.rearrange("p (s e) -> p s e", e=8)
            COEF = SOFT[:, 192:256]
            m4 = SM[:, 0:32]; m43 = m4.rearrange("p (s e) -> p s e", e=4)
            m2 = SM[:, 32:48]; m23 = m2.rearrange("p (s e) -> p s e", e=2)
            m8 = SM[:, 48:56]
            m8b = SM[:, 48:56].rearrange("p (s e) -> p s e", e=1).broadcast_to([128, 8, 8])
            s8 = SM[:, 56:64]
            rs8 = SM[:, 64:72]
            rt8 = SM[:, 72:80]
            w88 = SM[:, 80:88]
            w88b = SM[:, 80:88].rearrange("p (s e) -> p s e", e=1).broadcast_to([128, 8, 8])
            me8 = SM[:, 88:96]
            me8b = SM[:, 88:96].rearrange("p (s e) -> p s e", e=1).broadcast_to([128, 8, 8])
            ve("tensor_tensor", out=m43, in0=L3[:, :, 0:4], in1=L3[:, :, 4:8],
               op=ALU.max)
            ve("tensor_tensor", out=m23, in0=m43[:, :, 0:2], in1=m43[:, :, 2:4],
               op=ALU.max)
            ve("tensor_tensor", out=m8.rearrange("p (s e) -> p s e", e=1),
               in0=m23[:, :, 0:1], in1=m23[:, :, 1:2], op=ALU.max)
            ve("tensor_tensor", out=L3, in0=L3, in1=m8b, op=ALU.subtract)
            emit("scalar", "activation", out=L64, in_=L64, func=AF.Exp,
                 wait=[(dv, dv.n)], inc=[(ac, 1)])
            exp_v = ac.n
            ve("tensor_tensor", out=m43, in0=L3[:, :, 0:4], in1=L3[:, :, 4:8],
               op=ALU.add, wait=[(ac, exp_v)])
            ve("tensor_tensor", out=m23, in0=m43[:, :, 0:2], in1=m43[:, :, 2:4],
               op=ALU.add)
            ve("tensor_tensor", out=s8.rearrange("p (s e) -> p s e", e=1),
               in0=m23[:, :, 0:1], in1=m23[:, :, 1:2], op=ALU.add)
            ve("tensor_copy", out=rt8, in_=s8.bitcast(I32))
            ve("tensor_scalar", out=rt8, in0=rt8, scalar1=-1.0,
               scalar2=RMAGIC, op0=ALU.mult, op1=ALU.add)
            ve("tensor_copy", out=rs8.bitcast(I32), in_=rt8)
            for _i in range(2):
                ve("tensor_tensor", out=rt8, in0=s8, in1=rs8, op=ALU.mult)
                ve("tensor_scalar", out=rt8, in0=rt8, scalar1=-1.0, scalar2=2.0,
                   op0=ALU.mult, op1=ALU.add)
                ve("tensor_tensor", out=rs8, in0=rs8, in1=rt8, op=ALU.mult)
            ve("tensor_tensor", out=m43, in0=L3[:, :, 0:4], in1=L3[:, :, 4:8],
               op=ALU.max)
            ve("tensor_tensor", out=m23, in0=m43[:, :, 0:2], in1=m43[:, :, 2:4],
               op=ALU.max)
            ve("tensor_tensor", out=me8.rearrange("p (s e) -> p s e", e=1),
               in0=m23[:, :, 0:1], in1=m23[:, :, 1:2], op=ALU.max)
            ve("tensor_tensor", out=CMP3, in0=L3, in1=me8b, op=ALU.is_ge)
            ve("tensor_copy", out=CA, in_=CMP)
            ve("tensor_copy", out=CB, in_=CA)
            ve("tensor_tensor", out=CB3[:, :, 1:8], in0=CA3[:, :, 0:7],
               in1=CA3[:, :, 1:8], op=ALU.add)
            ve("tensor_copy", out=CA, in_=CB)
            ve("tensor_tensor", out=CA3[:, :, 2:8], in0=CB3[:, :, 0:6],
               in1=CB3[:, :, 2:8], op=ALU.add)
            ve("tensor_copy", out=CB, in_=CA)
            ve("tensor_tensor", out=CB3[:, :, 4:8], in0=CA3[:, :, 0:4],
               in1=CA3[:, :, 4:8], op=ALU.add)
            ve("tensor_scalar", out=CA, in0=CB, scalar1=-1.0, scalar2=2.0,
               op0=ALU.mult, op1=ALU.add)
            ve("tensor_scalar", out=CA, in0=CA, scalar1=1.0, scalar2=0.0,
               op0=ALU.min, op1=ALU.max)
            ve("tensor_tensor", out=CA, in0=CMP, in1=CA, op=ALU.mult)
            ve("tensor_scalar", out=w88, in0=rs8, scalar1=1.0 / 64.0,
               scalar2=None, op0=ALU.mult)
            ve("tensor_tensor", out=COEF.rearrange("p (s e) -> p s e", e=8),
               in0=CA3, in1=w88b, op=ALU.mult)
            route_v = dv.n

            for half in range(2):
                for s4 in range(4):
                    sg = half * 4 + s4
                    toff = sg * 128
                    for fc in range(DCn):
                        w = pdw if fc == 0 else None
                        pdw = []
                        emit("tensor", "matmul", PD[:, 0:512],
                             LN1B[:, fc * TT + toff: fc * TT + toff + 128],
                             ACAT[:, fc * 512:(fc + 1) * 512],
                             start=(fc == 0), stop=(fc == DCn - 1), wait=w,
                             inc=[(pe, 1)] if fc == DCn - 1 else None)
                    h_v = pe.n
                    hp = HPR[s4 % 2]
                    for e in range(8):
                        emit("vector", "tensor_scalar",
                             out=hp[:, e * 64:(e + 1) * 64],
                             in0=PD[:, e * 64:(e + 1) * 64],
                             scalar1=COEF[:, sg * 8 + e:sg * 8 + e + 1],
                             scalar2=None, op0=ALU.mult,
                             wait=([(pe, h_v), (dv, route_v)] + hpr_guard[s4 % 2])
                                  if e == 0 else None,
                             inc=[(dv, 1)] if e == 7 else None)
                        hpr_guard[s4 % 2] = []
                    hp_v = dv.n
                    pdw = [(dv, hp_v)]
                    trdst = [PA[:, 0:128], PA[:, 512:640],
                             PB[:, 0:128], PB[:, 512:640]]
                    for c in range(4):
                        w = None
                        if c == 0:
                            w = [(dv, hp_v)] + tr_bank_guard
                        emit("tensor", "matmul", trdst[c],
                             hp[:, c * 128:(c + 1) * 128], IDB[:],
                             start=True, stop=True, wait=w,
                             inc=[(pe, 1)] if c == 3 else None)
                    tr_v = pe.n
                    hpr_guard[s4 % 2] = [(pe, tr_v)]
                    for c in range(4):
                        emit("vector", "tensor_copy",
                             out=HT[:, c * 512 + s4 * 128: c * 512 + s4 * 128 + 128],
                             in_=trdst[c],
                             wait=([(pe, tr_v)] + (ht_guard if s4 == 0 else []))
                                  if c == 0 else None,
                             inc=[(dv, 1)] if c == 3 else None)
                    ht_guard = []
                    tr_bank_guard = [(dv, dv.n)]
                ht_v = dv.n
                banks = [(PA, 0, 0), (PA, 512, 1), (PB, 0, 2), (PB, 512, 3)]
                for dc in range(DCn):
                    pbk, boff, bid = banks[dc % 4]
                    bw = bank_guard.get(bid, []) + (tr_bank_guard if dc == 0 else [])
                    bank_guard[bid] = []
                    for er in range(4):
                        w = ([(dv, ht_v)] + bw) if er == 0 else None
                        emit("tensor", "matmul", pbk[:, boff:boff + 512],
                             BCAT[:, er * 2048 + dc * 128: er * 2048 + (dc + 1) * 128],
                             HT[:, er * 512:(er + 1) * 512],
                             start=(er == 0), stop=(er == 3), wait=w,
                             inc=[(pe, 1)] if er == 3 else None)
                    bv = pe.n
                    emit("vector", "tensor_tensor",
                         out=attn_c(dc)[:, half * 512:(half + 1) * 512],
                         in0=pbk[:, boff:boff + 512],
                         in1=attn_c(dc)[:, half * 512:(half + 1) * 512],
                         op=ALU.add, wait=[(pe, bv)], inc=[(dv, 1)])
                    final_done.append(dv.n)
                    bank_guard[bid] = [(dv, dv.n)]
                    ssem = st2 if dc < 8 else st
                    emit("sync" if half == 1 else "scalar", "dma_start",
                         out=out_d[:, tile * TT:(tile + 1) * TT].rearrange(
                             "(fc p) t -> p fc t", p=128)[:, dc:dc + 1,
                             half * 512:(half + 1) * 512],
                         in_=ATTN[:, dc * TT + half * 512:dc * TT
                                  + (half + 1) * 512].rearrange(
                             "p (fc t) -> p fc t", fc=1),
                         wait=[(dv, dv.n)], inc=[(ssem, 16)])
                ht_guard = [(pe, pe.n)]
                tr_bank_guard = (bank_guard[0] + bank_guard[1] + bank_guard[2]
                                 + bank_guard[3])
            guards["PA"] = bank_guard[0] + bank_guard[1]
            guards["PB"] = bank_guard[2] + bank_guard[3]
            guards["PC"] = pcw
            guards["PD"] = pdw

            guards["S2"] = [(st2, st2.n)]
            guards["ATTN"] = [(st, st.n), (st2, st2.n)]
            guards["S3"] = [(st, st.n)]

        emit_wait("sync", [(st, st.n), (st2, st2.n)])

        with nc.Block() as block:
            for eng in ["sync", "gpsimd", "vector", "scalar", "tensor"]:
                fns = [f for e, f in ORDER if e == eng]
                def runner(E, fns=fns):
                    for f in fns:
                        f(E)
                getattr(block, eng)(runner)
    return nc


def _stage_weights(inputs):
    f32 = np.float32
    Wk = np.asarray(inputs["L0_attn_Wk"], f32)
    Wr = np.asarray(inputs["L0_attn_Wr"], f32)
    Wg = np.asarray(inputs["L0_attn_Wg"], f32)
    Wv = np.asarray(inputs["L0_attn_Wv"], f32)
    A = np.asarray(inputs["L0_ffn_A"], f32)
    Bw = np.asarray(inputs["L0_ffn_B"], f32)
    WrF = np.asarray(inputs["L0_ffn_Wr"], f32)

    def projst(W):
        return np.ascontiguousarray(
            W.reshape(HCn, 128, DCn, 128).transpose(0, 3, 2, 1).reshape(HCn, 128, D))

    wvst = np.ascontiguousarray(
        Wv.reshape(DCn, 128, HCn, 128).transpose(0, 3, 2, 1).reshape(DCn, 128, H))
    acat = np.ascontiguousarray(
        A.transpose(2, 0, 1).reshape(DCn, 128, 512).transpose(1, 0, 2)
        .reshape(128, DCn * 512))
    Bp = Bw.transpose(0, 2, 1).reshape(512, 2048)
    bcat = np.ascontiguousarray(
        Bp.reshape(4, 128, 2048).transpose(1, 0, 2).reshape(128, 4 * 2048))
    wrt = np.ascontiguousarray(
        WrF.T.reshape(DCn, 128, 8).transpose(1, 0, 2).reshape(128, 128))

    def vec16(v):
        return np.ascontiguousarray(np.asarray(v, f32).reshape(DCn, 128).T)
    def vec14(v):
        return np.ascontiguousarray(np.asarray(v, f32).reshape(HCn, 128).T)

    return dict(
        wkst=projst(Wk), wrst=projst(Wr), wgst=projst(Wg), wvst=wvst,
        acat=acat, bcat=bcat, wrt=wrt,
        brf=np.ascontiguousarray(np.asarray(inputs["L0_ffn_br"], f32).reshape(1, 8)),
        tmk=vec16(inputs["L0_attn_time_mix_k"]),
        tmr=vec16(inputs["L0_attn_time_mix_r"]),
        tmw=vec16(inputs["L0_attn_time_mix_w"]),
        lnw=vec16(inputs["L0_attn_ln_w"]),
        lnb=vec16(inputs["L0_attn_ln_b"]),
        td=vec14(inputs["L0_attn_time_decay"]),
        idb=np.eye(128, dtype=ml_dtypes.bfloat16),
        idf=np.eye(128, dtype=np.float32),
        onesb=np.ones((128, 128), dtype=ml_dtypes.bfloat16),
    )


def kernel(**inputs):
    x = np.asarray(inputs["L0_attn_x"], np.float32)
    shared = _stage_weights(inputs)
    nc = build_program()

    in_maps = []
    for core in range(8):
        b, half = core // 2, core % 2
        t0 = half * NCORE
        xc = np.zeros((D, HALO + NCORE), np.float32)
        xc[:, HALO:] = x[b, t0:t0 + NCORE, :].T
        if half == 1:
            xc[:, 0:HALO] = x[b, t0 - HALO:t0, :].T
            mask = np.ones((128, 1), np.float32)
        else:
            xc[:, HALO - 1:HALO] = x[b, 1:2, :].T
            mask = np.zeros((128, 1), np.float32)
        m = dict(shared)
        m["xin"] = np.ascontiguousarray(xc)
        m["mask"] = mask
        in_maps.append(m)

    res = run_bass_kernel_spmd(nc, in_maps, core_ids=list(range(8)))
    out = np.empty((B, TFULL, D), np.float32)
    for core in range(8):
        b, half = core // 2, core % 2
        t0 = half * NCORE
        out[b, t0:t0 + NCORE, :] = res.results[core]["out"].T
    return out
